# version 01 (final)
# baseline (speedup 1.0000x reference)
import numpy as np
import concourse.bass as bass
import concourse.mybir as mybir
from concourse.bass_utils import run_bass_kernel_spmd
from concourse.bass import IndirectOffsetOnAxis

F32 = mybir.dt.float32
BF16 = mybir.dt.bfloat16
I32 = mybir.dt.int32
Alu = mybir.AluOpType
AF = mybir.ActivationFunctionType
AX = mybir.AxisListType

T_CORE = 8192
D = 1024
NT = 16
TT = 512
EPS = 1e-5
K_LOC = 1280
NEXP = 8
NITER = 46
BIG = 1.0e30
NEG = -3.0e38
POS = 3.0e38
GROUPS = [(0, 512), (512, 512), (1024, 256)]
SUBS_PER_EXP = sum(r // 128 for _, r in GROUPS)
NTILE = 64


def _sub_base(g):
    return sum(GROUPS[gg][1] // 128 for gg in range(g))


def build_phase_a(stage=6):
    S_ON = stage >= 3 or stage in (20, 21, 22, 23)
    SOFT_ON = stage >= 3 or stage in (22, 23)
    PT_ON = stage >= 3 or stage == 23
    AV_ON = stage >= 3 and stage < 20
    nc = bass.Bass("TRN2", target_bir_lowering=False, debug=False, detect_race_conditions=False)
    dt = nc.dram_tensor
    x_tm = dt("x_tm", [T_CORE, D], F32, kind="ExternalInput").ap()
    x_fmT = dt("x_fmT", [D, T_CORE], F32, kind="ExternalInput").ap()
    w_qk = dt("w_qk", [D, 2048], F32, kind="ExternalInput").ap()
    w_v = dt("w_v", [D, 1024], F32, kind="ExternalInput").ap()
    w_o = dt("w_o", [D, 1024], F32, kind="ExternalInput").ap()
    b_qk = dt("b_qk", [128, 16], F32, kind="ExternalInput").ap()
    bv_rep = dt("bv_rep", [128, 1024], F32, kind="ExternalInput").ap()
    bo_rep = dt("bo_rep", [128, 1024], F32, kind="ExternalInput").ap()
    g_rep = dt("g_rep", [128, 1024], F32, kind="ExternalInput").ap()
    bln_rep = dt("bln_rep", [128, 1024], F32, kind="ExternalInput").ap()
    centT = dt("centT", [D, 8], F32, kind="ExternalInput").ap()
    g_ch = dt("g_ch", [128, 8], F32, kind="ExternalInput").ap()
    bln_ch = dt("bln_ch", [128, 8], F32, kind="ExternalInput").ap()
    ident = dt("ident", [128, 128], F32, kind="ExternalInput").ap()

    x1_out = dt("x1", [T_CORE, D], F32, kind="ExternalOutput").ap()
    norm_out = dt("norm_bf", [T_CORE, D], BF16, kind="ExternalOutput").ap()
    scT_out = dt("scoresT", [8, T_CORE], F32, kind="ExternalOutput").ap()

    from contextlib import ExitStack
    ctx = ExitStack()
    ctx.enter_context(nc.cleanup_on_exit())

    def sbuf(name, shape, d):
        return ctx.enter_context(nc.sbuf_tensor(name, shape, d))

    def psum(name, shape, d):
        return ctx.enter_context(nc.psum_tensor(name, shape, d))

    wqk_sb = sbuf("wqk_sb", [128, 8 * 2048], BF16)
    wv_sb = sbuf("wv_sb", [128, 8 * 1024], BF16)
    wo_sb = sbuf("wo_sb", [128, 8 * 1024], BF16)
    bqk_sb = sbuf("bqk_sb", [128, 16], F32)
    bv_sb = sbuf("bv_sb", [128, 1024], F32)
    bo_sb = sbuf("bo_sb", [128, 1024], F32)
    grep_sb = sbuf("grep_sb", [128, 1024], F32)
    blnr_sb = sbuf("blnr_sb", [128, 1024], F32)
    gc_sb = sbuf("gc_sb", [128, 8 * 8], F32)
    centT_sb = sbuf("centT_sb", [128, 8 * 8], F32)
    gch_sb = sbuf("gch_sb", [128, 8], F32)
    blnch_sb = sbuf("blnch_sb", [128, 8], F32)
    idb_sb = sbuf("idb_sb", [128, 128], BF16)
    idf_sb = sbuf("idf_sb", [128, 128], F32)
    bc_sb = sbuf("bc_sb", [128, 1], F32)

    xfm_sb = sbuf("xfm_sb", [128, 8 * TT], BF16)
    xtm_sb = sbuf("xtm_sb", [128, 2 * 1024], F32)
    xpb_sb = sbuf("xpb_sb", [128, 2 * 1024], F32)
    qk_sb = sbuf("qk_sb", [128, 16 * TT], BF16)
    v_sb = sbuf("v_sb", [128, 4 * 1024], BF16)
    p3_sb = sbuf("p3_sb", [128, 1024], F32)
    p5_sb = sbuf("p5_sb", [128, 1024], F32)
    p4_sb = sbuf("p4_sb", [128, 1024], BF16)
    pt_sb = sbuf("pt_sb", [128, 1024], BF16)
    vals_sb = sbuf("vals_sb", [128, 8 * TT], BF16)
    y_sb = sbuf("y_sb", [128, 1024], F32)
    x1_sb = sbuf("x1_sb", [128, 2 * 1024], F32)
    xn_sb = sbuf("xn_sb", [128, 1024], F32)
    nf_sb = sbuf("nf_sb", [128, 4 * 1024], F32)
    nb_sb = sbuf("nb_sb", [128, 2 * 1024], BF16)
    nfm_sb = sbuf("nfm_sb", [128, 8 * TT], F32)
    sct_sb = sbuf("sct_sb", [128, 2 * TT], F32)
    st_sb = sbuf("st_sb", [128, 64], F32)
    fill_sb = sbuf("fill_sb", [128, 16], F32)

    _gapstate = [0]

    def gap(e, n=2):
        for _ in range(n):
            k = _gapstate[0] % 4
            _gapstate[0] += 1
            e.tensor_copy(fill_sb[:, 2 * k:2 * k + 1], fill_sb[:, 2 * k + 1:2 * k + 2])

    ps_qk0 = psum("ps_qk0", [128, 512], F32)
    ps_qk1 = psum("ps_qk1", [128, 512], F32)
    ps_v = psum("ps_v", [128, 512], F32)
    ps_s2 = psum("ps_s2", [128, 1024], F32)
    ps_ptb = psum("ps_ptb", [128, 512], BF16)
    ps_vals = psum("ps_vals", [128, 512], F32)
    ps_sc = psum("ps_sc", [128, 512], F32)

    block = ctx.enter_context(nc.Block())

    sems = {}
    for nm in ("wld", "xfm0", "xfm1", "xtm0", "xtm1", "qkmm", "qkev", "vmm", "vev", "smm", "sub",
               "exp", "pn", "ptmm", "ptev", "avmm", "valev", "owmm", "xpb",
               "rs1", "rs2", "x1", "nrm", "scmm", "sctv", "tpose", "tpev",
               "bcmm", "yev", "sw", "stx1_0", "stx1_1", "stnb_0", "stnb_1", "stsc_0", "stsc_1"):
        sems[nm] = ctx.enter_context(nc.semaphore("s_" + nm))

    S = sems
    N_WLD_DMAS = 8 + 8 + 8 + 1 + 1 + 5 + 1 + 8 + 1

    @block.sync
    def _(e):
        if stage >= 4:
            e.dma_start(out=xtm_sb[:, 0:1024], in_=x_tm[0:128, :]).then_inc(S["xtm0"], 16)
            e.dma_start(out=xtm_sb[:, 1024:2048], in_=x_tm[128:256, :]).then_inc(S["xtm1"], 16)
        for i in range(4 * NT):
            t = i // 4
            if i >= 1 and stage >= 4:
                ip = i - 1
                e.wait_ge(S["x1"], ip + 1)
                e.dma_start(out=x1_out[ip * 128:(ip + 1) * 128, :],
                            in_=x1_sb[:, (ip % 2) * 1024:((ip % 2) + 1) * 1024]).then_inc(S["stx1_%d" % (ip % 2)], 16)
                e.wait_ge(S["nrm"], ip + 1)
                e.dma_start(out=norm_out[ip * 128:(ip + 1) * 128, :],
                            in_=nb_sb[:, (ip % 2) * 1024:((ip % 2) + 1) * 1024]).then_inc(S["stnb_%d" % (ip % 2)], 16)
            if i % 4 == 0 and t >= 1 and stage >= 6:
                tp = t - 1
                e.wait_ge(S["sctv"], tp + 2)
                e.dma_start(out=scT_out[:, tp * TT:(tp + 1) * TT],
                            in_=sct_sb[0:8, (tp % 2) * TT:((tp % 2) + 1) * TT]).then_inc(S["stsc_%d" % (tp % 2)], 16)
            if i + 2 < 4 * NT and stage >= 4:
                j = i + 2
                e.wait_ge(S["xpb"], j - 1)
                e.dma_start(out=xtm_sb[:, (j % 2) * 1024:((j % 2) + 1) * 1024],
                            in_=x_tm[j * 128:(j + 1) * 128, :]).then_inc(S["xtm%d" % (j % 2)], 16)
        if stage >= 4:
            e.wait_ge(S["x1"], 4 * NT)
            e.dma_start(out=x1_out[(4 * NT - 1) * 128:4 * NT * 128, :], in_=x1_sb[:, 1024:2048]).then_inc(S["stx1_1"], 16)
            e.wait_ge(S["nrm"], 4 * NT)
            e.dma_start(out=norm_out[(4 * NT - 1) * 128:4 * NT * 128, :], in_=nb_sb[:, 1024:2048]).then_inc(S["stnb_1"], 16)
        if stage >= 6:
            e.wait_ge(S["sctv"], NT + 1)
            e.dma_start(out=scT_out[:, (NT - 1) * TT:NT * TT],
                        in_=sct_sb[0:8, ((NT - 1) % 2) * TT:(((NT - 1) % 2) + 1) * TT]).then_inc(S["stsc_%d" % ((NT - 1) % 2)], 16)
        if stage >= 4:
            e.wait_ge(S["stx1_0"], 16 * 2 * NT)
            e.wait_ge(S["stx1_1"], 16 * 2 * NT)
            e.wait_ge(S["stnb_0"], 16 * 2 * NT)
            e.wait_ge(S["stnb_1"], 16 * 2 * NT)
        if stage >= 6:
            e.wait_ge(S["stsc_0"], 16 * ((NT + 1) // 2))
            e.wait_ge(S["stsc_1"], 16 * (NT // 2))
        if stage == 1:
            e.wait_ge(S["qkev"], 16 * NT)
        elif stage == 2:
            e.wait_ge(S["vev"], 4 * NT)
        elif stage in (20, 21):
            e.wait_ge(S["sub"], 16 * NT)
        elif stage == 22:
            e.wait_ge(S["pn"], 16 * NT)
        elif stage == 23:
            e.wait_ge(S["ptev"], 16 * NT)
        elif stage == 3:
            e.wait_ge(S["valev"], 8 * NT)
        elif stage == 4 or stage == 5:
            pass

    @block.gpsimd
    def _(e):
        for c in range(8):
            e.dma_start(out=wqk_sb[:, c * 2048:(c + 1) * 2048],
                        in_=w_qk[c * 128:(c + 1) * 128, :]).then_inc(S["wld"], 16)
        for c in range(8):
            e.dma_start(out=wv_sb[:, c * 1024:(c + 1) * 1024],
                        in_=w_v[c * 128:(c + 1) * 128, :]).then_inc(S["wld"], 16)
        for c in range(8):
            e.dma_start(out=wo_sb[:, c * 1024:(c + 1) * 1024],
                        in_=w_o[c * 128:(c + 1) * 128, :]).then_inc(S["wld"], 16)
        e.dma_start(out=idb_sb[:, :], in_=ident[:, :]).then_inc(S["wld"], 16)
        e.dma_start(out=idf_sb[:, :], in_=ident[:, :]).then_inc(S["wld"], 16)
        for (dst, src) in ((bqk_sb, b_qk), (bv_sb, bv_rep), (bo_sb, bo_rep),
                           (grep_sb, g_rep), (blnr_sb, bln_rep)):
            e.dma_start(out=dst[:, :], in_=src[:, :]).then_inc(S["wld"], 16)
        e.dma_start(out=gch_sb[:, :], in_=g_ch[:, :]).then_inc(S["wld"], 16)
        for c in range(8):
            e.dma_start(out=centT_sb[:, c * 8:(c + 1) * 8],
                        in_=centT[c * 128:(c + 1) * 128, :]).then_inc(S["wld"], 16)
        e.dma_start(out=blnch_sb[:, :], in_=bln_ch[:, :]).then_inc(S["wld"], 16)

        def xfm_load(t):
            if t >= 1:
                if stage >= 2:
                    e.wait_ge(S["vmm"], 8 * t)
                else:
                    e.wait_ge(S["qkmm"], 16 * t)
            for c in range(8):
                e.dma_start(out=xfm_sb[:, c * TT:(c + 1) * TT],
                            in_=x_fmT[c * 128:(c + 1) * 128, t * TT:(t + 1) * TT]).then_inc(S["xfm0"], 16)

        xfm_load(0)
        e.wait_ge(S["wld"], 16 * N_WLD_DMAS)
        for t in range(NT):
            for q in range(4 if stage >= 4 else 0):
                i = 4 * t + q
                e.wait_ge(S["xtm%d" % (i % 2)], 16 * (i // 2 + 1))
                if i >= 2:
                    e.wait_ge(S["yev"], i - 1)
                buf = (i % 2) * 1024
                e.tensor_tensor(xpb_sb[:, buf:buf + 1024], xtm_sb[:, buf:buf + 1024],
                                bo_sb[:, :], Alu.add).then_inc(S["xpb"], 1)
            if t + 1 < NT:
                xfm_load(t + 1)

    @block.tensor
    def _(e):
        sw_cnt = [0]
        e.wait_ge(S["wld"], 16 * N_WLD_DMAS)
        for c in range(8):
            ins = e.matmul(ps_sc[0:8, 0:1], centT_sb[:, c * 8:(c + 1) * 8],
                           blnch_sb[:, c:c + 1], start=(c == 0), stop=(c == 7))
        ins.then_inc(S["bcmm"], 1)
        for t in range(NT):
            tb = 0
            e.wait_ge(S["xfm0"], 128 * (t + 1))
            if t >= 1 and stage >= 4:
                e.wait_ge(S["yev"], 4 * t)
            if t >= 1 and stage == 1:
                e.wait_ge(S["qkev"], 16 * t)
            for fc in range(16):
                bank = ps_qk0 if fc % 2 == 0 else ps_qk1
                if 16 * t + fc >= 2:
                    e.wait_ge(S["qkev"], 16 * t + fc - 1)
                for c in range(8):
                    ins = e.matmul(bank[:, :], wqk_sb[:, c * 2048 + fc * 128:c * 2048 + (fc + 1) * 128],
                                   xfm_sb[:, tb + c * TT:tb + (c + 1) * TT],
                                   start=(c == 0), stop=(c == 7))
                ins.then_inc(S["qkmm"], 1)
            for q in range(4 if stage >= 2 else 0):
                for h in range(2):
                    vh = 8 * t + 2 * q + h
                    if vh >= 1:
                        e.wait_ge(S["vev"], vh)
                    for c in range(8):
                        ins = e.matmul(ps_v[:, :],
                                       xfm_sb[:, tb + c * TT + q * 128:tb + c * TT + (q + 1) * 128],
                                       wv_sb[:, c * 1024 + h * 512:c * 1024 + (h + 1) * 512],
                                       start=(c == 0), stop=(c == 7))
                    ins.then_inc(S["vmm"], 1)
            for j in range(8 if S_ON else 0):
                pj = 8 * t + j
                e.wait_ge(S["qkev"], 16 * t + 8 + j + 1)
                if pj >= 1:
                    e.wait_ge(S["sub"], 2 * pj)
                if t >= 1 and stage >= 6:
                    e.wait_ge(S["tpev"], 8 * t)
                for h2 in range(2):
                    base = 64 * h2
                    for bp in range(4):
                        ins = e.matmul(ps_s2[:, 512 * h2 + 128 * bp:512 * h2 + 128 * (bp + 1)],
                                       qk_sb[base:base + 64, TT * j + 128 * bp:TT * j + 128 * (bp + 1)],
                                       qk_sb[base:base + 64, TT * (8 + j) + 128 * bp:TT * (8 + j) + 128 * (bp + 1)],
                                       start=True, stop=True)
                    if h2 == 0:
                        ins.then_inc(S["sw"], 1)
                        sw_cnt[0] += 1
                        e.wait_ge(S["sw"], sw_cnt[0])
                ins.then_inc(S["smm"], 1)
                if not PT_ON:
                    continue
                e.wait_ge(S["pn"], 2 * (pj + 1))
                if pj >= 1:
                    e.wait_ge(S["ptev"], 2 * pj)
                for r in range(2):
                    for k4 in range(4):
                        sslab = 4 * r + k4
                        ins = e.transpose(ps_ptb[:, 128 * k4:128 * (k4 + 1)],
                                          p4_sb[:, 128 * sslab:128 * (sslab + 1)], idb_sb[:, :])
                    ins.then_inc(S["ptmm"], 1)
                    if r == 0:
                        e.wait_ge(S["ptev"], 2 * pj + 1)
                if not AV_ON:
                    continue
                e.wait_ge(S["ptev"], 2 * (pj + 1))
                if pj >= 1:
                    e.wait_ge(S["valev"], pj)
                for h2 in range(2):
                    h = 2 * j + h2
                    for bp in range(4):
                        ins = e.matmul(ps_vals[64 * h2:64 * h2 + 64, 128 * bp:128 * (bp + 1)],
                                       v_sb[:, 1024 * bp + 64 * h:1024 * bp + 64 * (h + 1)],
                                       pt_sb[:, 128 * (4 * h2 + bp):128 * (4 * h2 + bp + 1)],
                                       start=True, stop=True)
                    ins.then_inc(S["sw"], 1)
                    sw_cnt[0] += 1
                    e.wait_ge(S["sw"], sw_cnt[0])
                e.sem_inc(S["avmm"], 1)
            if stage < 4:
                continue
            e.wait_ge(S["valev"], 8 * (t + 1))
            e.wait_ge(S["qkev"], 16 * (t + 1))
            for q in range(4):
                iq = 4 * t + q
                if iq >= 1:
                    e.wait_ge(S["yev"], iq)
                for h in range(2):
                    bank = ps_qk0 if h == 0 else ps_qk1
                    for jj in range(8):
                        ins = e.matmul(bank[:, :], vals_sb[:, TT * jj + 128 * q:TT * jj + 128 * (q + 1)],
                                       wo_sb[:, 1024 * jj + 512 * h:1024 * jj + 512 * (h + 1)],
                                       start=(jj == 0), stop=(jj == 7))
                ins.then_inc(S["owmm"], 1)
            if stage < 6:
                continue
            for q in range(4):
                iq = 4 * t + q
                e.wait_ge(S["nrm"], iq + 1)
                for r in range(2):
                    R = 8 * t + 2 * q + r
                    if R >= 1:
                        e.wait_ge(S["tpev"], R)
                    for k in range(4):
                        c = 4 * r + k
                        ins = e.transpose(ps_s2[:, 128 * k:128 * (k + 1)],
                                          nf_sb[:, 1024 * q + 128 * c:1024 * q + 128 * (c + 1)], idf_sb[:, :])
                    ins.then_inc(S["tpose"], 1)
            e.wait_ge(S["tpev"], 8 * (t + 1))
            e.wait_ge(S["sctv"], t + 1)
            for c in range(8):
                ins = e.matmul(ps_sc[0:8, :], centT_sb[:, c * 8:(c + 1) * 8],
                               nfm_sb[:, TT * c:TT * (c + 1)], start=(c == 0), stop=(c == 7))
            ins.then_inc(S["scmm"], 1)

    @block.scalar
    def _(e):
        for t in range(NT):
            if t >= 1 and stage >= 3:
                e.wait_ge(S["smm"], 8 * t)
            for fc in range(16):
                e.wait_ge(S["qkmm"], 16 * t + fc + 1)
                bank = ps_qk0 if fc % 2 == 0 else ps_qk1
                e.activation(out=qk_sb[:, TT * fc:TT * (fc + 1)], in_=bank[:, :],
                             func=AF.Identity, bias=bqk_sb[:, fc:fc + 1], scale=1.0).then_inc(S["qkev"], 1)
            for j in range(8 if SOFT_ON else 0):
                pj = 8 * t + j
                lo_i = p3_sb[0:64, :].rearrange("p (g k) -> p g k", k=128)[:, :, 0:64]
                lo_o = p5_sb[0:64, :].rearrange("p (g k) -> p g k", k=128)[:, :, 0:64]
                hi_i = p3_sb[64:128, :].rearrange("p (g k) -> p g k", k=128)[:, :, 64:128]
                hi_o = p5_sb[64:128, :].rearrange("p (g k) -> p g k", k=128)[:, :, 64:128]
                e.wait_ge(S["sub"], 2 * pj + 1)
                e.activation(out=lo_o, in_=lo_i, func=AF.Exp, scale=0.125).then_inc(S["exp"], 1)
                e.wait_ge(S["sub"], 2 * pj + 2)
                e.activation(out=hi_o, in_=hi_i, func=AF.Exp, scale=0.125).then_inc(S["exp"], 1)
            for q in range(4 if stage >= 4 else 0):
                iq = 4 * t + q
                e.wait_ge(S["rs1"], iq + 1)
                e.activation(out=st_sb[:, 40:42], in_=st_sb[:, 36:38],
                             func=AF.Sqrt).then_inc(S["rs2"], 1)

    @block.vector
    def _(e):
        e.memset(fill_sb[:, :], 0.0)
        e.memset(p4_sb[:, :], 0.0)
        e.wait_ge(S["wld"], 16 * N_WLD_DMAS)
        for c in range(8):
            e.tensor_scalar(gc_sb[:, c * 8:(c + 1) * 8], centT_sb[:, c * 8:(c + 1) * 8],
                            gch_sb[:, c:c + 1], None, Alu.mult, Alu.bypass)
        e.wait_ge(S["bcmm"], 1)
        e.tensor_copy(bc_sb[0:8, :], ps_sc[0:8, 0:1]).then_inc(S["sctv"], 1)
        for t in range(NT):
            for q in range(4 if stage >= 2 else 0):
                for h in range(2):
                    vh = 8 * t + 2 * q + h
                    e.wait_ge(S["vmm"], vh + 1)
                    if t >= 1 and q == 0 and h == 0 and AV_ON:
                        e.wait_ge(S["avmm"], 8 * t)
                    e.scalar_tensor_tensor(v_sb[:, 1024 * q + 512 * h:1024 * q + 512 * (h + 1)],
                                           ps_v[:, :], 1.0,
                                           bv_sb[:, 512 * h:512 * (h + 1)],
                                           Alu.mult, Alu.add).then_inc(S["vev"], 1)
            for j in range(8 if S_ON else 0):
                pj = 8 * t + j
                slo = ps_s2[0:64, :].rearrange("p (g k) -> p g k", k=128)[:, :, 0:64]
                shi = ps_s2[64:128, :].rearrange("p (g k) -> p g k", k=128)[:, :, 64:128]
                p3lo = p3_sb[0:64, :].rearrange("p (g k) -> p g k", k=128)[:, :, 0:64]
                p3hi = p3_sb[64:128, :].rearrange("p (g k) -> p g k", k=128)[:, :, 64:128]
                p5lo = p5_sb[0:64, :].rearrange("p (g k) -> p g k", k=128)[:, :, 0:64]
                p5hi = p5_sb[64:128, :].rearrange("p (g k) -> p g k", k=128)[:, :, 64:128]
                p4lo = p4_sb[0:64, :].rearrange("p (g k) -> p g k", k=128)[:, :, 0:64]
                p4hi = p4_sb[64:128, :].rearrange("p (g k) -> p g k", k=128)[:, :, 64:128]
                e.wait_ge(S["smm"], pj + 1)
                if stage == 20:
                    e.tensor_copy(p3_sb[:, :], ps_s2[:, :]).then_inc(S["sub"], 2)
                    continue
                e.reduce_max(st_sb[0:64, 0:8], slo, axis=AX.X, negate=True)
                e.reduce_max(st_sb[64:128, 0:8], shi, axis=AX.X, negate=True)
                gap(e)
                e.tensor_tensor(p3lo, slo,
                                st_sb[0:64, 0:8].unsqueeze(2).broadcast_to([64, 8, 64]),
                                Alu.add).then_inc(S["sub"], 1)
                e.tensor_tensor(p3hi, shi,
                                st_sb[64:128, 0:8].unsqueeze(2).broadcast_to([64, 8, 64]),
                                Alu.add).then_inc(S["sub"], 1)
                if not SOFT_ON:
                    continue
                e.wait_ge(S["exp"], 2 * (pj + 1))
                e.reduce_sum(st_sb[0:64, 8:16], p5lo, axis=AX.X)
                e.reduce_sum(st_sb[64:128, 8:16], p5hi, axis=AX.X)
                gap(e, 1)
                e.reciprocal(st_sb[:, 16:24], st_sb[:, 8:16])
                gap(e)
                e.tensor_tensor(p4lo, p5lo,
                                st_sb[0:64, 16:24].unsqueeze(2).broadcast_to([64, 8, 64]),
                                Alu.mult).then_inc(S["pn"], 1)
                e.tensor_tensor(p4hi, p5hi,
                                st_sb[64:128, 16:24].unsqueeze(2).broadcast_to([64, 8, 64]),
                                Alu.mult).then_inc(S["pn"], 1)
                if not PT_ON:
                    continue
                for r in range(2):
                    e.wait_ge(S["ptmm"], 2 * pj + r + 1)
                    e.tensor_copy(pt_sb[:, 512 * r:512 * (r + 1)], ps_ptb[:, :]).then_inc(S["ptev"], 1)
                if not AV_ON:
                    continue
                e.wait_ge(S["avmm"], pj + 1)
                e.tensor_copy(vals_sb[:, TT * j:TT * (j + 1)], ps_vals[:, :]).then_inc(S["valev"], 1)
            for q in range(4 if stage >= 4 else 0):
                iq = 4 * t + q
                buf = (iq % 2) * 1024
                e.wait_ge(S["owmm"], iq + 1)
                e.wait_ge(S["xpb"], iq + 1)
                if t >= 1:
                    e.wait_ge(S["tpose"], 8 * (t - 1) + 2 * q + 2)
                if iq >= 2:
                    e.wait_ge(S["stx1_%d" % (iq % 2)], 16 * (iq // 2))
                    e.wait_ge(S["stnb_%d" % (iq % 2)], 16 * (iq // 2))
                e.scalar_tensor_tensor(y_sb[:, 0:512], ps_qk0[:, :], 1.0,
                                       xpb_sb[:, buf:buf + 512], Alu.mult, Alu.add)
                e.scalar_tensor_tensor(y_sb[:, 512:1024], ps_qk1[:, :], 1.0,
                                       xpb_sb[:, buf + 512:buf + 1024], Alu.mult, Alu.add).then_inc(S["yev"], 1)
                e.tensor_tensor(xn_sb[:, :], y_sb[:, :], y_sb[:, :], Alu.mult)
                e.reduce_sum(st_sb[:, 24:25], y_sb[:, :], axis=AX.X)
                e.reduce_sum(st_sb[:, 25:26], xn_sb[:, :], axis=AX.X)
                e.tensor_scalar(st_sb[:, 26:27], st_sb[:, 24:25], 1.0 / 1024, None, Alu.mult, Alu.bypass)
                e.tensor_scalar(st_sb[:, 28:29], st_sb[:, 25:26], 1.0 / 1024, None, Alu.mult, Alu.bypass)
                gap(e, 1)
                e.tensor_tensor(st_sb[:, 27:28], st_sb[:, 26:27], st_sb[:, 26:27], Alu.mult)
                gap(e)
                e.tensor_tensor(st_sb[:, 29:30], st_sb[:, 28:29], st_sb[:, 27:28], Alu.subtract)
                gap(e)
                e.tensor_scalar(st_sb[:, 30:31], st_sb[:, 29:30], EPS, None, Alu.add, Alu.bypass)
                gap(e)
                e.reciprocal(st_sb[:, 31:32], st_sb[:, 30:31])
                gap(e)
                e.tensor_tensor(st_sb[:, 32:33], st_sb[:, 29:30], st_sb[:, 31:32], Alu.mult)
                gap(e)
                e.tensor_scalar(st_sb[:, 33:34], st_sb[:, 32:33], EPS, None, Alu.add, Alu.bypass)
                gap(e)
                e.reciprocal(st_sb[:, 34:35], st_sb[:, 33:34])
                e.tensor_copy(st_sb[:, 36:37], st_sb[:, 31:32])
                gap(e, 1)
                e.tensor_copy(st_sb[:, 37:38], st_sb[:, 34:35]).then_inc(S["rs1"], 1)
                e.wait_ge(S["rs2"], iq + 1)
                gap(e)
                e.tensor_scalar(x1_sb[:, buf:buf + 1024], y_sb[:, :],
                                st_sb[:, 26:27], st_sb[:, 40:41],
                                Alu.subtract, Alu.mult).then_inc(S["x1"], 1)
                gap(e)
                e.tensor_scalar(xn_sb[:, :], x1_sb[:, buf:buf + 1024],
                                st_sb[:, 41:42], None, Alu.mult, Alu.bypass)
                gap(e)
                e.tensor_tensor(nf_sb[:, 1024 * q:1024 * (q + 1)], xn_sb[:, :], grep_sb[:, :], Alu.mult)
                gap(e)
                e.tensor_tensor(nf_sb[:, 1024 * q:1024 * (q + 1)], nf_sb[:, 1024 * q:1024 * (q + 1)], blnr_sb[:, :], Alu.add)
                gap(e)
                e.tensor_copy(nb_sb[:, buf:buf + 1024], nf_sb[:, 1024 * q:1024 * (q + 1)]).then_inc(S["nrm"], 1)
            for q in range(4 if stage >= 6 else 0):
                for r in range(2):
                    R = 8 * t + 2 * q + r
                    e.wait_ge(S["tpose"], R + 1)
                    for k in range(4):
                        c = 4 * r + k
                        ins = e.tensor_copy(nfm_sb[:, TT * c + 128 * q:TT * c + 128 * (q + 1)],
                                            ps_s2[:, 128 * k:128 * (k + 1)])
                    ins.then_inc(S["tpev"], 1)
            if stage < 6:
                continue
            e.wait_ge(S["scmm"], t + 1)
            if t >= 2:
                e.wait_ge(S["stsc_%d" % (t % 2)], 16 * (t // 2))
            e.tensor_copy(sct_sb[0:8, (t % 2) * TT:((t % 2) + 1) * TT],
                          ps_sc[0:8, :]).then_inc(S["sctv"], 1)

    ctx.close()
    return nc




def make_inputs_a(x_shard, w_qkv, b_qkv, w_o, b_o, cents, ln_g, ln_b):
    perm_q = np.concatenate([np.arange(192 * h, 192 * h + 64) for h in range(16)])
    perm_k = perm_q + 64
    perm_v = perm_q + 128
    d = {}
    d["x_tm"] = np.ascontiguousarray(x_shard)
    d["x_fmT"] = np.ascontiguousarray(x_shard.T)
    d["w_qk"] = np.ascontiguousarray(w_qkv[:, np.concatenate([perm_q, perm_k])])
    d["w_v"] = np.ascontiguousarray(w_qkv[:, perm_v])
    d["w_o"] = np.ascontiguousarray(w_o)
    d["b_qk"] = np.ascontiguousarray(b_qkv[np.concatenate([perm_q, perm_k])].reshape(16, 128).T)
    d["bv_rep"] = np.ascontiguousarray(np.tile(b_qkv[perm_v][None, :], (128, 1)))
    d["bo_rep"] = np.ascontiguousarray(np.tile(b_o[None, :], (128, 1)))
    d["g_rep"] = np.ascontiguousarray(np.tile(ln_g[None, :], (128, 1)))
    d["bln_rep"] = np.ascontiguousarray(np.tile(ln_b[None, :], (128, 1)))
    d["centT"] = np.ascontiguousarray(cents.T)
    d["g_ch"] = np.ascontiguousarray(ln_g.reshape(8, 128).T)
    d["bln_ch"] = np.ascontiguousarray(ln_b.reshape(8, 128).T)
    d["ident"] = np.eye(128, dtype=np.float32)
    return d


def build_phase_b():
    nc = bass.Bass("TRN2", target_bir_lowering=False, debug=False, detect_race_conditions=False)
    dt = nc.dram_tensor
    x1 = dt("x1", [T_CORE, D], F32, kind="ExternalInput").ap()
    norm_bf = dt("norm_bf", [T_CORE, D], BF16, kind="ExternalInput").ap()
    scT_all = dt("scT_all", [8 * 128, 512], F32, kind="ExternalInput").ap()
    w1 = dt("w1", [NEXP * 1024, 4096], F32, kind="ExternalInput").ap()
    w2 = dt("w2", [NEXP * 4096, 1024], F32, kind="ExternalInput").ap()
    b1ch = dt("b1ch", [128, NEXP * 32], F32, kind="ExternalInput").ap()
    b2rep = dt("b2rep", [NEXP * 128, 1024], F32, kind="ExternalInput").ap()
    ones128 = dt("ones128", [128, 128], F32, kind="ExternalInput").ap()
    ltri = dt("ltri", [128, 128], F32, kind="ExternalInput").ap()
    lloc = dt("lloc", [128, 128], F32, kind="ExternalInput").ap()
    selmat = dt("selmat", [128, 16], F32, kind="ExternalInput").ap()
    ident = dt("ident", [128, 128], F32, kind="ExternalInput").ap()
    out = dt("out", [T_CORE, D], F32, kind="ExternalOutput").ap()


    scat = dt("scat", [NEXP * K_LOC, D], BF16, kind="Internal").ap()
    hbuf = dt("hbuf", [NEXP * K_LOC, D], F32, kind="Internal").ap()

    from contextlib import ExitStack
    ctx = ExitStack()
    ctx.enter_context(nc.cleanup_on_exit())

    def sbuf(name, shape, d):
        return ctx.enter_context(nc.sbuf_tensor(name, shape, d))

    def psum(name, shape, d):
        return ctx.enter_context(nc.psum_tensor(name, shape, d))

    w1_sb = sbuf("w1_sb", [128, 8 * 4096], BF16)
    w2_sb = sbuf("w2_sb", [128, 32 * 1024], BF16)
    h1g_sb = sbuf("h1g_sb", [128, 32 * 512], BF16)
    nfm_sb = sbuf("nfm_sb", [128, 8 * 512], BF16)
    b1_sb = sbuf("b1_sb", [128, NEXP * 32], F32)
    b2_sb = sbuf("b2_sb", [128, 1024], F32)
    ones_sb = sbuf("ones_sb", [128, 128], F32)
    ltri_sb = sbuf("ltri_sb", [128, 128], F32)
    lloc_sb = sbuf("lloc_sb", [128, 128], F32)
    selm_sb = sbuf("selm_sb", [128, 16], F32)
    idf_sb = sbuf("idf_sb", [128, 128], F32)

    keys_sb = sbuf("keys_sb", [128, 512], F32)
    avail_sb = sbuf("avail_sb", [128, 512], F32)
    junk_sb = sbuf("junk_sb", [128, 512], F32)
    c1_sb = sbuf("c1_sb", [128, 512], F32)
    keym_sb = c1_sb
    sel_sb = sbuf("sel_sb", [128, 512], F32)
    g1_sb = sbuf("g1_sb", [128, 512], F32)
    doff_sb = sbuf("doff_sb", [128, 512], F32)
    zero_sb = sbuf("zero_sb", [128, 512], F32)
    mine_sb = sbuf("mine_sb", [16, 512], F32)
    idxf_sb = sbuf("idxf_sb", [128, 64], F32)
    idx_sb = sbuf("idx_sb", [128, 64], I32)
    st_sb = sbuf("st_sb", [128, 64], F32)
    fill_sb = sbuf("fill_sb", [128, 16], F32)

    nrm_sb = sbuf("nrm_sb", [128, 2 * 1024], BF16)
    h2_sb = sbuf("h2_sb", [128, 2 * 1024], F32)
    xt_sb = w1_sb[:, 0:2048].bitcast(F32)
    ht_sb = w1_sb[:, 2048:4096].bitcast(F32)
    y2_sb = w1_sb[:, 4096:6144].bitcast(F32)
    ot_sb = w1_sb[:, 6144:10240].bitcast(F32)

    ps_tot = psum("ps_tot", [128, 512], F32)
    ps_w1a = psum("ps_w1a", [128, 512], F32)
    ps_w1b = psum("ps_w1b", [128, 512], F32)
    ps_out = psum("ps_out", [128, 1024], F32)

    block = ctx.enter_context(nc.Block())

    sems = {}
    for nm in ("cst", "keyl", "b2l", "cnt", "tot", "scn", "exg", "exl", "sel",
               "dofe", "mine", "offs", "idx", "nld0", "nld1", "sct0", "sct1",
               "wexp", "nfm", "w1mm", "gel", "w2mm", "h2ev", "hst0", "hst1",
               "gth", "xld", "ost0", "ost1", "fin", "rs1", "rs2"):
        sems[nm] = ctx.enter_context(nc.semaphore("s_" + nm))
    S = sems

    _gapstate = [0]

    def gap(e, n=2):
        for _ in range(n):
            k = _gapstate[0] % 4
            _gapstate[0] += 1
            e.tensor_copy(fill_sb[:, 2 * k:2 * k + 1], fill_sb[:, 2 * k + 1:2 * k + 2])

    def keys_src(ee):
        return scT_all[128 * ee:128 * (ee + 1), :]

    N_CST = 7

    @block.sync
    def _(e):
        for (dst, src) in ((ones_sb, ones128), (ltri_sb, ltri), (lloc_sb, lloc),
                           (selm_sb, selmat), (idf_sb, ident), (b1_sb, b1ch)):
            e.dma_start(out=dst[:, :], in_=src[:, :]).then_inc(S["cst"], 16)
        e.dma_start(out=keys_sb[:, :], in_=keys_src(0)).then_inc(S["cst"], 16)
        for ee in range(1, NEXP - 1):
            e.wait_ge(S["sel"], ee)
            e.dma_start(out=keys_sb[:, :], in_=keys_src(ee)).then_inc(S["keyl"], 16)
        e.wait_ge(S["idx"], 1)
        for i in range(NTILE):
            b = i % 2
            if i >= 2:
                e.wait_ge(S["sct%d" % b], 16 * (i // 2))
            e.dma_start(out=nrm_sb[:, b * 1024:(b + 1) * 1024],
                        in_=norm_bf[i * 128:(i + 1) * 128, :]).then_inc(S["nld%d" % b], 16)
        e.wait_ge(S["sct0"], 16 * 32)
        e.wait_ge(S["sct1"], 16 * 32)
        m = 0
        for ee in range(NEXP):
            e.wait_ge(S["h2ev"], SUBS_PER_EXP * ee)
            e.dma_start(out=b2_sb[:, :], in_=b2rep[ee * 128:(ee + 1) * 128, :]).then_inc(S["b2l"], 16)
            for g, (roff, rows) in enumerate(GROUPS):
                gi = ee * 3 + g
                nsub = rows // 128
                if gi >= 1:
                    e.wait_ge(S["w1mm"], 32 * gi)
                for c in range(8):
                    for s in range(nsub):
                        r0 = ee * K_LOC + roff + 128 * s
                        e.dma_start(out=nfm_sb[:, 512 * c + 128 * s:512 * c + 128 * (s + 1)],
                                    in_=scat[r0:r0 + 128, 128 * c:128 * (c + 1)],
                                    transpose=True).then_inc(S["nfm"], 16)
                for s in range(nsub):
                    hb = m % 2
                    e.wait_ge(S["h2ev"], m + 1)
                    r0 = ee * K_LOC + roff + 128 * s
                    e.dma_start(out=hbuf[r0:r0 + 128, :],
                                in_=h2_sb[:, hb * 1024:(hb + 1) * 1024]).then_inc(S["hst%d" % hb], 16)
                    m += 1
        for i in range(NTILE):
            b = i % 2
            if i >= 1:
                e.wait_ge(S["fin"], i)
            e.dma_start(out=xt_sb[:, :], in_=x1[i * 128:(i + 1) * 128, :]).then_inc(S["xld"], 16)
            e.wait_ge(S["fin"], i + 1)
            e.dma_start(out=out[i * 128:(i + 1) * 128, :],
                        in_=ot_sb[:, b * 1024:(b + 1) * 1024]).then_inc(S["ost%d" % b], 16)
        e.wait_ge(S["ost0"], 16 * 32)
        e.wait_ge(S["ost1"], 16 * 32)

    @block.gpsimd
    def _(e):
        def wload(ee):
            if ee >= 1:
                e.wait_ge(S["w2mm"], SUBS_PER_EXP * ee)
            for c in range(8):
                e.dma_start(out=w1_sb[:, 4096 * c:4096 * (c + 1)],
                            in_=w1[1024 * ee + 128 * c:1024 * ee + 128 * (c + 1), :]).then_inc(S["wexp"], 16)
            for hc in range(32):
                e.dma_start(out=w2_sb[:, 1024 * hc:1024 * (hc + 1)],
                            in_=w2[4096 * ee + 128 * hc:4096 * ee + 128 * (hc + 1), :]).then_inc(S["wexp"], 16)

        wload(0)
        e.wait_ge(S["idx"], 1)
        for i in range(NTILE):
            b = i % 2
            col = 16 * (i % 4) + i // 4
            e.wait_ge(S["nld%d" % b], 16 * (i // 2 + 1))
            e.indirect_dma_start(scat[:, :], IndirectOffsetOnAxis(idx_sb[:, col:col + 1], 0),
                                 nrm_sb[:, b * 1024:(b + 1) * 1024], None).then_inc(S["sct%d" % b], 16)
        for ee in range(1, NEXP):
            wload(ee)
        e.wait_ge(S["hst0"], 16 * 40)
        e.wait_ge(S["hst1"], 16 * 40)
        for i in range(NTILE):
            col = 16 * (i % 4) + i // 4
            if i >= 1:
                e.wait_ge(S["fin"], i)
            e.indirect_dma_start(ht_sb[:, :], None, hbuf[:, :],
                                 IndirectOffsetOnAxis(idx_sb[:, col:col + 1], 0)).then_inc(S["gth"], 16)

    @block.tensor
    def _(e):
        e.wait_ge(S["cst"], 16 * N_CST)
        for ee in range(NEXP - 1):
            for it in range(NITER + 1):
                k = ee * (NITER + 1) + it
                e.wait_ge(S["cnt"], k + 1)
                e.matmul(ps_tot[:, 0:1], ones_sb[:, :], st_sb[:, 8:9],
                         start=True, stop=True).then_inc(S["tot"], 1)
            e.wait_ge(S["scn"], 2 * ee + 1)
            e.matmul(ps_tot[:, 1:2], ltri_sb[:, :], st_sb[:, 9:10],
                     start=True, stop=True).then_inc(S["exg"], 1)
            e.wait_ge(S["scn"], 2 * ee + 2)
            e.matmul(ps_tot[:, 2:3], lloc_sb[:, :], st_sb[:, 10:11],
                     start=True, stop=True).then_inc(S["exl"], 1)
        e.wait_ge(S["scn"], 16)
        e.matmul(ps_tot[:, 2:3], lloc_sb[:, :], st_sb[:, 10:11],
                 start=True, stop=True).then_inc(S["exl"], 1)
        e.wait_ge(S["dofe"], 1)
        e.matmul(ps_tot[0:16, :], selm_sb[:, :], doff_sb[:, :],
                 start=True, stop=True).then_inc(S["mine"], 1)
        e.wait_ge(S["mine"], 2)
        for bb in range(4):
            ins = e.transpose(ps_tot[:, 16 * bb:16 * (bb + 1)],
                              mine_sb[0:16, 128 * bb:128 * (bb + 1)], idf_sb[0:16, 0:16])
        ins.then_inc(S["offs"], 1)
        nfm_cnt = 0
        m = 0
        for ee in range(NEXP):
            e.wait_ge(S["wexp"], 16 * 40 * (ee + 1))
            for g, (roff, rows) in enumerate(GROUPS):
                gi = ee * 3 + g
                nsub = rows // 128
                nfm_cnt += 8 * nsub
                e.wait_ge(S["nfm"], 16 * nfm_cnt)
                for hc in range(32):
                    bank = ps_w1a if hc % 2 == 0 else ps_w1b
                    if gi * 32 + hc >= 2:
                        e.wait_ge(S["gel"], gi * 32 + hc - 1)
                    for c in range(8):
                        ins = e.matmul(bank[:, 0:rows],
                                       w1_sb[:, 4096 * c + 128 * hc:4096 * c + 128 * (hc + 1)],
                                       nfm_sb[:, 512 * c:512 * c + rows],
                                       start=(c == 0), stop=(c == 7))
                    ins.then_inc(S["w1mm"], 1)
                e.wait_ge(S["gel"], 32 * (gi + 1))
                for s in range(nsub):
                    if m >= 1:
                        e.wait_ge(S["h2ev"], m)
                    for hc in range(32):
                        for hh in range(2):
                            ins = e.matmul(ps_out[:, 512 * hh:512 * (hh + 1)],
                                           h1g_sb[:, 512 * hc + 128 * s:512 * hc + 128 * s + 128],
                                           w2_sb[:, 1024 * hc + 512 * hh:1024 * hc + 512 * (hh + 1)],
                                           start=(hc == 0), stop=(hc == 31))
                    ins.then_inc(S["w2mm"], 1)
                    m += 1

    @block.scalar
    def _(e):
        for ee in range(NEXP):
            for g, (roff, rows) in enumerate(GROUPS):
                gi = ee * 3 + g
                if gi >= 1:
                    e.wait_ge(S["w2mm"], SUBS_PER_EXP * (gi // 3) + _sub_base(gi % 3))
                for hc in range(32):
                    bank = ps_w1a if hc % 2 == 0 else ps_w1b
                    e.wait_ge(S["w1mm"], gi * 32 + hc + 1)
                    e.activation(out=h1g_sb[:, 512 * hc:512 * hc + rows], in_=bank[:, 0:rows],
                                 func=AF.Gelu_apprx_tanh,
                                 bias=b1_sb[:, 32 * ee + hc:32 * ee + hc + 1],
                                 scale=1.0).then_inc(S["gel"], 1)
        for i in range(NTILE):
            e.wait_ge(S["rs1"], i + 1)
            e.activation(out=st_sb[:, 40:41], in_=st_sb[:, 36:37],
                         func=AF.Sqrt).then_inc(S["rs2"], 1)

    @block.vector
    def _(e):
        e.memset(fill_sb[:, :], 0.0)
        e.memset(zero_sb[:, :], 0.0)
        e.memset(avail_sb[:, :], 1.0)
        e.memset(doff_sb[:, :], 0.0)
        e.memset(st_sb[:, :], 0.0)
        e.wait_ge(S["cst"], 16 * N_CST)
        for ee in range(NEXP):
            if ee >= 1 and ee < NEXP - 1:
                e.wait_ge(S["keyl"], 16 * ee)
            if ee < 7:
                e.tensor_scalar(junk_sb[:, :], avail_sb[:, :], BIG, BIG, Alu.mult, Alu.subtract)
                e.tensor_tensor(keym_sb[:, :], keys_sb[:, :], avail_sb[:, :], Alu.mult)
                gap(e)
                e.tensor_tensor(keym_sb[:, :], keym_sb[:, :], junk_sb[:, :], Alu.add)
                e.reduce_max(st_sb[:, 8:9], keys_sb[:, :], axis=AX.X,
                             apply_absolute_value=True)
                gap(e)
                e.tensor_copy(st_sb[:, 9:10], st_sb[:, 8:9]).then_inc(S["cnt"], 1)
                e.wait_ge(S["tot"], ee * (NITER + 1) + 1)
                e.tensor_scalar(st_sb[:, 1:2], ps_tot[:, 0:1], 1.0, None, Alu.add, Alu.bypass)
                e.tensor_scalar(st_sb[:, 0:1], ps_tot[:, 0:1], -1.0, 1.0, Alu.mult, Alu.subtract)
                gap(e)
                for it in range(NITER):
                    k = ee * (NITER + 1) + it + 1
                    e.tensor_tensor(st_sb[:, 2:3], st_sb[:, 1:2], st_sb[:, 0:1], Alu.subtract)
                    gap(e, 1)
                    e.tensor_scalar(st_sb[:, 3:4], st_sb[:, 2:3], 0.5, None, Alu.mult, Alu.bypass)
                    gap(e, 1)
                    e.tensor_tensor(st_sb[:, 4:5], st_sb[:, 0:1], st_sb[:, 3:4], Alu.add)
                    gap(e)
                    e.tensor_tensor(junk_sb[:, :], keym_sb[:, :],
                                    st_sb[:, 4:5].broadcast_to([128, 512]), Alu.is_gt)
                    e.reduce_sum(st_sb[:, 8:9], junk_sb[:, :], axis=AX.X).then_inc(S["cnt"], 1)
                    e.wait_ge(S["tot"], k + 1)
                    e.tensor_scalar(st_sb[:, 5:6], ps_tot[:, 0:1], 8192.0, None, Alu.is_ge, Alu.bypass)
                    gap(e)
                    e.tensor_tensor(st_sb[:, 6:7], st_sb[:, 5:6], st_sb[:, 3:4], Alu.mult)
                    gap(e, 1)
                    e.tensor_tensor(st_sb[:, 0:1], st_sb[:, 0:1], st_sb[:, 6:7], Alu.add)
                    e.tensor_tensor(st_sb[:, 7:8], st_sb[:, 3:4], st_sb[:, 6:7], Alu.subtract)
                    gap(e, 1)
                    e.tensor_tensor(st_sb[:, 1:2], st_sb[:, 1:2], st_sb[:, 7:8], Alu.subtract)
                    gap(e, 6)
                gap(e, 4)
                e.tensor_tensor(sel_sb[:, :], keym_sb[:, :],
                                st_sb[:, 1:2].broadcast_to([128, 512]), Alu.is_ge)
                e.tensor_tensor_scan(c1_sb[:, :], sel_sb[:, :], zero_sb[:, :], 0.0, Alu.add, Alu.add)
                gap(e)
                e.tensor_copy(st_sb[:, 9:10], c1_sb[:, 511:512]).then_inc(S["scn"], 1)
                e.wait_ge(S["exg"], ee + 1)
                gap(e)
                e.tensor_scalar(g1_sb[:, :], c1_sb[:, :], ps_tot[:, 1:2], None, Alu.add, Alu.bypass)
                gap(e)
                e.tensor_tensor(g1_sb[:, :], g1_sb[:, :], sel_sb[:, :], Alu.subtract)
                gap(e)
                e.tensor_scalar(g1_sb[:, :], g1_sb[:, :], 8192.0, None, Alu.is_lt, Alu.bypass)
                gap(e)
                e.tensor_tensor(sel_sb[:, :], sel_sb[:, :], g1_sb[:, :], Alu.mult)
                gap(e)
            else:
                e.tensor_copy(sel_sb[:, :], avail_sb[:, :])
                e.sem_inc(S["scn"], 1)
            e.tensor_tensor_scan(c1_sb[:, :], sel_sb[:, :], zero_sb[:, :], 0.0, Alu.add, Alu.add)
            gap(e)
            e.tensor_copy(st_sb[:, 10:11], c1_sb[:, 511:512]).then_inc(S["scn"], 1)
            e.wait_ge(S["exl"], ee + 1)
            gap(e)
            e.tensor_scalar(g1_sb[:, :], c1_sb[:, :], ps_tot[:, 2:3], None, Alu.add, Alu.bypass)
            gap(e)
            e.tensor_tensor(g1_sb[:, :], g1_sb[:, :], sel_sb[:, :], Alu.subtract)
            e.tensor_scalar(g1_sb[:, :], g1_sb[:, :], float(K_LOC - 1), None, Alu.min, Alu.bypass)
            e.tensor_scalar(g1_sb[:, :], g1_sb[:, :], float(K_LOC * ee), None, Alu.add, Alu.bypass)
            gap(e)
            e.tensor_tensor(g1_sb[:, :], g1_sb[:, :], sel_sb[:, :], Alu.mult)
            gap(e)
            e.tensor_tensor(doff_sb[:, :], doff_sb[:, :], g1_sb[:, :], Alu.add)
            e.tensor_tensor(avail_sb[:, :], avail_sb[:, :], sel_sb[:, :], Alu.subtract).then_inc(S["sel"], 1)
        e.tensor_copy(fill_sb[:, 8:9], doff_sb[:, 0:1]).then_inc(S["dofe"], 1)
        e.wait_ge(S["mine"], 1)
        e.tensor_copy(mine_sb[0:16, :], ps_tot[0:16, :]).then_inc(S["mine"], 1)
        e.wait_ge(S["offs"], 1)
        e.tensor_copy(idxf_sb[:, :], ps_tot[:, 0:64])
        gap(e)
        e.tensor_copy(idx_sb[:, :], idxf_sb[:, :]).then_inc(S["idx"], 1)
        m = 0
        for ee in range(NEXP):
            e.wait_ge(S["b2l"], 16 * (ee + 1))
            for g, (roff, rows) in enumerate(GROUPS):
                nsub = rows // 128
                for s in range(nsub):
                    hb = m % 2
                    e.wait_ge(S["w2mm"], m + 1)
                    if m >= 2:
                        e.wait_ge(S["hst%d" % hb], 16 * (m // 2))
                    e.scalar_tensor_tensor(h2_sb[:, hb * 1024:hb * 1024 + 1024],
                                           ps_out[:, :], 1.0, b2_sb[:, :],
                                           Alu.mult, Alu.add).then_inc(S["h2ev"], 1)
                    m += 1
        for i in range(NTILE):
            b = i % 2
            e.wait_ge(S["gth"], 16 * (i + 1))
            e.wait_ge(S["xld"], 16 * (i + 1))
            if i >= 2:
                e.wait_ge(S["ost%d" % b], 16 * (i // 2))
            e.scalar_tensor_tensor(y2_sb[:, :], xt_sb[:, :], 2.0, ht_sb[:, :],
                                   Alu.mult, Alu.add)
            e.tensor_tensor(junk_sb[:, :], y2_sb[:, 0:512], y2_sb[:, 0:512], Alu.mult)
            e.tensor_tensor(g1_sb[:, :], y2_sb[:, 512:1024], y2_sb[:, 512:1024], Alu.mult)
            e.reduce_sum(st_sb[:, 24:25], y2_sb[:, :], axis=AX.X)
            e.reduce_sum(st_sb[:, 25:26], junk_sb[:, :], axis=AX.X)
            e.reduce_sum(st_sb[:, 26:27], g1_sb[:, :], axis=AX.X)
            e.tensor_scalar(st_sb[:, 27:28], st_sb[:, 24:25], 1.0 / 1024, None, Alu.mult, Alu.bypass)
            e.tensor_tensor(st_sb[:, 28:29], st_sb[:, 25:26], st_sb[:, 26:27], Alu.add)
            gap(e, 1)
            e.tensor_tensor(st_sb[:, 29:30], st_sb[:, 27:28], st_sb[:, 27:28], Alu.mult)
            e.tensor_scalar(st_sb[:, 30:31], st_sb[:, 28:29], 1.0 / 1024, None, Alu.mult, Alu.bypass)
            gap(e, 1)
            e.tensor_tensor(st_sb[:, 31:32], st_sb[:, 30:31], st_sb[:, 29:30], Alu.subtract)
            gap(e)
            e.tensor_scalar(st_sb[:, 32:33], st_sb[:, 31:32], EPS, None, Alu.add, Alu.bypass)
            gap(e)
            e.reciprocal(st_sb[:, 36:37], st_sb[:, 32:33]).then_inc(S["rs1"], 1)
            e.wait_ge(S["rs2"], i + 1)
            gap(e)
            e.tensor_scalar(ot_sb[:, b * 1024:(b + 1) * 1024], y2_sb[:, :],
                            st_sb[:, 27:28], st_sb[:, 40:41],
                            Alu.subtract, Alu.mult).then_inc(S["fin"], 1)

    ctx.close()
    return nc




def make_inputs_b(core_id, x1, norm_bf, scT_all, w1, b1, w2, b2):
    d = {}
    d["x1"] = np.ascontiguousarray(x1)
    d["norm_bf"] = np.ascontiguousarray(norm_bf)
    d["scT_all"] = np.ascontiguousarray(
        scT_all.reshape(8, 8, 16, 512).transpose(1, 0, 2, 3).reshape(8 * 128, 512))
    d["w1"] = np.ascontiguousarray(w1.reshape(NEXP * 1024, 4096))
    d["w2"] = np.ascontiguousarray(w2.reshape(NEXP * 4096, 1024))
    d["b1ch"] = np.ascontiguousarray(b1.reshape(NEXP, 32, 128).transpose(2, 0, 1).reshape(128, NEXP * 32))
    d["b2rep"] = np.ascontiguousarray(np.broadcast_to(b2[:, None, :], (NEXP, 128, 1024)).reshape(NEXP * 128, 1024))
    d["ones128"] = np.ones((128, 128), np.float32)
    d["ltri"] = np.ascontiguousarray(np.triu(np.ones((128, 128), np.float32), 1))
    q = np.arange(128)
    d["lloc"] = np.ascontiguousarray(((q[:, None] < q[None, :]) & (q[:, None] // 16 == q[None, :] // 16)).astype(np.float32))
    sm = np.zeros((128, 16), np.float32)
    for i in range(16):
        sm[16 * core_id + i, i] = 1.0
    d["selmat"] = sm
    d["ident"] = np.eye(128, dtype=np.float32)
    return d


_CACHE = {}


def kernel(**inputs):
    x = np.ascontiguousarray(np.asarray(inputs["L0_attn_x"], dtype=np.float32).reshape(65536, 1024))
    w_qkv = np.asarray(inputs["L0_attn_w_qkv"], dtype=np.float32)
    b_qkv = np.asarray(inputs["L0_attn_b_qkv"], dtype=np.float32)
    w_o = np.asarray(inputs["L0_attn_w_o"], dtype=np.float32)
    b_o = np.asarray(inputs["L0_attn_b_o"], dtype=np.float32)
    cents = np.asarray(inputs["L0_ffn_centroids"], dtype=np.float32)
    ln_g = np.asarray(inputs["L0_ffn_ln_g"], dtype=np.float32)
    ln_b = np.asarray(inputs["L0_ffn_ln_b"], dtype=np.float32)
    w1 = np.asarray(inputs["L0_ffn_w1"], dtype=np.float32)
    b1 = np.asarray(inputs["L0_ffn_b1"], dtype=np.float32)
    w2 = np.asarray(inputs["L0_ffn_w2"], dtype=np.float32)
    b2 = np.asarray(inputs["L0_ffn_b2"], dtype=np.float32)

    if "a" not in _CACHE:
        _CACHE["a"] = build_phase_a()
        _CACHE["b"] = build_phase_b()
    nca, ncb = _CACHE["a"], _CACHE["b"]

    in_a = [make_inputs_a(x[c * T_CORE:(c + 1) * T_CORE], w_qkv, b_qkv, w_o, b_o,
                          cents, ln_g, ln_b) for c in range(8)]
    res_a = run_bass_kernel_spmd(nca, in_a, core_ids=list(range(8)))
    scT_all = np.concatenate([res_a.results[c]["scoresT"] for c in range(8)], axis=0)
    in_b = [make_inputs_b(c, res_a.results[c]["x1"], res_a.results[c]["norm_bf"],
                          scT_all, w1, b1, w2, b2) for c in range(8)]
    res_b = run_bass_kernel_spmd(ncb, in_b, core_ids=list(range(8)))
    out = np.concatenate([res_b.results[c]["out"] for c in range(8)], axis=0)
    return out.reshape(16, 4096, 1024)
